# version 03
# baseline (speedup 1.0000x reference)
import math
import numpy as np
import ml_dtypes
import concourse.bass as bass
import concourse.mybir as mybir
from concourse.bass_utils import run_bass_kernel_spmd

F32 = mybir.dt.float32
BF16 = mybir.dt.bfloat16
AF = mybir.ActivationFunctionType
ALU = mybir.AluOpType
AX = mybir.AxisListType.X

B, S, D, H, HD = 2, 4096, 4096, 32, 128
TPC, NT, KCH = 1024, 8, 32
ATT_SCALE = 1.0 / math.sqrt(HD)
EPS = 1e-5
ENGS = ("tensor", "vector", "scalar", "gpsimd", "sync")


class Prog:
    def __init__(self):
        self.ops = {e: [] for e in ENGS}
        self.semval = {}
        self.hiwater = {e: {} for e in ENGS}
        self.res = {}
        self.eng_sem = {"tensor": "pe", "vector": "ve", "scalar": "se",
                        "gpsimd": "po", "sync": "sy"}
        for s in self.eng_sem.values():
            self.mksem(s)

    def mksem(self, name):
        self.semval.setdefault(name, 0)

    def _need(self, eng, waits, dep):
        if dep is None:
            return
        sem, val = dep
        if val <= 0:
            return
        if eng == "tensor" and sem == "pe":
            return
        hw = self.hiwater[eng]
        if hw.get(sem, 0) >= val:
            return
        waits.append((sem, val))
        hw[sem] = val

    def emit(self, eng, fn, reads=(), writes=(), dma_sem=None, extra=(), tag=""):
        waits = []
        for r in reads:
            st = self.res.get(r)
            if st is not None:
                self._need(eng, waits, st["w"])
        for w in writes:
            st = self.res.get(w)
            if st is not None:
                self._need(eng, waits, st["w"])
                for rd in st["r"]:
                    self._need(eng, waits, rd)
        for d in extra:
            self._need(eng, waits, d)
        if dma_sem is not None:
            self.mksem(dma_sem)
            prev = self.semval[dma_sem]
            if prev:
                self._need(eng, waits, (dma_sem, prev))
            self.semval[dma_sem] += 16
            comp = (dma_sem, self.semval[dma_sem])
            inc = (dma_sem, 16)
        else:
            es = self.eng_sem[eng]
            self.semval[es] += 1
            comp = (es, self.semval[es])
            inc = (es, 1)
        self.ops[eng].append((waits, fn, inc, tag))
        for w in writes:
            self.res[w] = {"w": comp, "r": []}
        for r in reads:
            st = self.res.setdefault(r, {"w": None, "r": []})
            st["r"].append(comp)
        return comp

    def barrier(self, eng, deps):
        waits = []
        for d in deps:
            self._need(eng, waits, d)
        if waits:
            self.ops[eng].append((waits, None, None, "barrier"))


def build_nc():
    nc = bass.Bass("TRN2", target_bir_lowering=False)

    x_in = nc.dram_tensor("x0", [TPC, D], F32, kind="ExternalInput").ap()
    wqk = {"a": nc.dram_tensor("wqk_a", [64, 128, KCH, 128], F32, kind="ExternalInput").ap(),
           "b": nc.dram_tensor("wqk_b", [64, 128, KCH, 128], F32, kind="ExternalInput").ap()}
    wv = {"a": nc.dram_tensor("wv_a", [16, 128, KCH, 256], F32, kind="ExternalInput").ap(),
          "b": nc.dram_tensor("wv_b", [16, 128, KCH, 256], F32, kind="ExternalInput").ap()}
    wo = {"a": nc.dram_tensor("wo_a", [16, 128, KCH, 256], F32, kind="ExternalInput").ap(),
          "b": nc.dram_tensor("wo_b", [16, 128, KCH, 256], F32, kind="ExternalInput").ap()}
    wup_t = nc.dram_tensor("wup", [16, 128, KCH, 256], F32, kind="ExternalInput").ap()
    bqk_in = {"a": nc.dram_tensor("bqk_a", [128, 64], F32, kind="ExternalInput").ap(),
              "b": nc.dram_tensor("bqk_b", [128, 64], F32, kind="ExternalInput").ap()}
    brow_in = {"a": nc.dram_tensor("brow_a", [2, D], F32, kind="ExternalInput").ap(),
               "b": nc.dram_tensor("brow_b", [2, D], F32, kind="ExternalInput").ap()}
    gaT_in = nc.dram_tensor("gaT_in", [128, KCH, 36], F32, kind="ExternalInput").ap()
    bsT_in = nc.dram_tensor("bsT_in", [32, D], F32, kind="ExternalInput").ap()
    eb_in = nc.dram_tensor("eb_in", [1, 4], F32, kind="ExternalInput").ap()
    cos_in = nc.dram_tensor("cos_in", [128, TPC], F32, kind="ExternalInput").ap()
    sin_in = nc.dram_tensor("sin_in", [128, TPC], F32, kind="ExternalInput").ap()
    perm_in = nc.dram_tensor("perm_in", [128, 128], F32, kind="ExternalInput").ap()
    mask_in = nc.dram_tensor("mask_in", [128, 4, 8, 256], F32, kind="ExternalInput").ap()
    ident_in = nc.dram_tensor("ident_in", [128, 128], F32, kind="ExternalInput").ap()
    out_t = nc.dram_tensor("xout", [TPC, D], F32, kind="ExternalOutput").ap()

    master = nc.dram_tensor("master", [TPC, D], F32, kind="Internal")
    q_dram = nc.dram_tensor("q_dram", [D, TPC], BF16, kind="Internal")
    contrib = nc.dram_tensor("contrib", [2 * D, TPC], BF16, kind="Internal")
    ag = {"a": nc.dram_tensor("ag_a", [8 * D, TPC], BF16, kind="Internal"),
          "b": nc.dram_tensor("ag_b", [8 * D, TPC], BF16, kind="Internal")}

    P = Prog()
    SY = [f"dsy{i}" for i in range(8)]
    GP = [f"dgp{i}" for i in range(6)]
    cnt = {"sy": 0, "gp": 0}

    def sy_s():
        cnt["sy"] += 1
        return SY[cnt["sy"] % len(SY)]

    def gp_s():
        cnt["gp"] += 1
        return GP[cnt["gp"] % len(GP)]

    import contextlib
    with contextlib.ExitStack() as st_:
        def sb(name, shape, dt):
            return st_.enter_context(nc.sbuf_tensor(name, shape, dt))

        big = sb("big", [128, KCH, TPC], BF16)
        wmv = sb("wmv", [128, 2, KCH, 256], BF16)
        xld = sb("xld", [128, 1, D], F32)
        ytm = sb("ytm", [128, D], BF16)
        rsc = sb("rsc", [128, 2, 512], F32)
        qkb = sb("qkb", [128, 2, 512], BF16)
        vev = sb("vev", [128, 2, 256], BF16)
        cosb = sb("cosb", [128, TPC], F32)
        sinb = sb("sinb", [128, TPC], F32)
        permB = sb("permB", [128, 128], BF16)
        qkpre = sb("qkpre", [128, 512], BF16)
        maskT = sb("maskT", [128, 4, 8, 256], BF16)
        Ksb = sb("Ksb", [128, 4, 4, 256], BF16)
        Vsb = sb("Vsb", [128, 4, 8, 128], BF16)
        qTs = sb("qTs", [128, TPC], BF16)
        PT = sb("PT", [128, 2, 1024], BF16)
        recB = sb("recB", [128, 256], F32)
        r1 = sb("r1", [1, 256], F32)
        stats = sb("stats", [128, 16], F32)
        lg = sb("lg", [128, 36], F32)
        mex = sb("mex", [128, 4], F32)
        zt = sb("zt", [128, 32], BF16)
        ztT = sb("ztT", [32, TPC], BF16)
        bsT = sb("bsT", [32, D], BF16)
        gaT = sb("gaT", [128, KCH, 36], BF16)
        bqk = {"a": sb("bqkA", [128, 64], F32), "b": sb("bqkB", [128, 64], F32)}
        brow_cur = sb("brow_cur", [1, D], BF16)
        ebr = sb("ebr", [1, 4], F32)
        ebc = sb("ebc", [128, 4], F32)
        onesc = sb("onesc", [128, 1], BF16)
        onesr = sb("onesr", [1, 128], BF16)
        onesrf = sb("onesrf", [1, 128], F32)
        ident = sb("ident", [128, 128], BF16)
        xoc = sb("xoc", [128, 2, 256], F32)
        xic = sb("xic", [128, 2, 256], F32)

        psABbig = st_.enter_context(nc.psum_tensor("psABbig", [128, 1024], F32))
        psSbig = st_.enter_context(nc.psum_tensor("psSbig", [128, 1024], F32))
        psCt = st_.enter_context(nc.psum_tensor("psCt", [128, 512], F32))
        psD = st_.enter_context(nc.psum_tensor("psD", [128, 512], F32))
        psT = st_.enter_context(nc.psum_tensor("psT", [128, 1024], BF16))
        psT2 = st_.enter_context(nc.psum_tensor("psT2", [128, 1024], BF16))

        sems = {}
        for name in ["pe", "ve", "se", "po", "sy", "cc1", "cc2"] + SY + GP:
            sems[name] = st_.enter_context(nc.semaphore(name))
        block = st_.enter_context(nc.Block())

        psAB = [psABbig[:, 0:512], psABbig[:, 512:1024]]
        psTT = [psT, psT2]

        P.emit("gpsimd", lambda g: g.dma_start(out=ident[:], in_=ident_in), writes=["ident"], dma_sem=gp_s())
        P.emit("gpsimd", lambda g: g.dma_start(out=permB[:], in_=perm_in), writes=["permB"], dma_sem=gp_s())
        P.emit("gpsimd", lambda g: g.dma_start(out=maskT[:], in_=mask_in), writes=["maskT"], dma_sem=gp_s())
        P.emit("gpsimd", lambda g: g.dma_start(out=gaT[:], in_=gaT_in), writes=["gaT"], dma_sem=gp_s())
        P.emit("gpsimd", lambda g: g.dma_start(out=bsT[:], in_=bsT_in), writes=["bsT"], dma_sem=gp_s())

        P.emit("sync", lambda s: s.dma_start(out=cosb[:], in_=cos_in), writes=["cosb"], dma_sem=sy_s())
        P.emit("sync", lambda s: s.dma_start(out=sinb[:], in_=sin_in), writes=["sinb"], dma_sem=sy_s())
        P.emit("sync", lambda s: s.dma_start(out=bqk["a"][:], in_=bqk_in["a"]), writes=["bqkA"], dma_sem=sy_s())
        P.emit("sync", lambda s: s.dma_start(out=bqk["b"][:], in_=bqk_in["b"]), writes=["bqkB"], dma_sem=sy_s())
        P.emit("sync", lambda s: s.dma_start(out=ebr[:], in_=eb_in), writes=["ebr"], dma_sem=sy_s())
        P.emit("vector", lambda v: v.memset(onesc[:], 1.0), writes=["onesc"])
        P.emit("vector", lambda v: v.memset(onesr[:], 1.0), writes=["onesr"])
        P.emit("vector", lambda v: v.memset(onesrf[:], 1.0), writes=["onesrf"])
        P.emit("tensor", lambda t: t.matmul(psD[:, 0:4], onesrf[:], ebr[:], start=True, stop=True),
               reads=["onesrf", "ebr"], writes=["psD"], tag="ebbc")
        P.emit("scalar", lambda sc: sc.copy(out=ebc[:], in_=psD[:, 0:4]), reads=["psD"], writes=["ebc", "psD"])

        mst_deps = {tt: [] for tt in range(NT)}

        def sublayer_barrier():
            deps = [(s, P.semval[s]) for s in SY + ["cc1", "cc2"] if P.semval.get(s, 0) > 0]
            P.barrier("sync", deps)

        def phase_y(sub):
            src = x_in if sub == 0 else master.ap()
            for tt in range(NT):
                sl = 0
                xres = "xld0"
                P.emit("sync",
                       lambda s, tt=tt, sl=sl, src=src: s.dma_start(out=xld[:, sl, :], in_=src[tt * 128:(tt + 1) * 128, :]),
                       writes=[xres], dma_sem=sy_s(), extra=mst_deps[tt], tag="xload")
                P.emit("vector", lambda v, sl=sl: v.reduce_sum(stats[:, 0:1], xld[:, sl, :], axis=AX),
                       reads=[xres], writes=["st0"])
                for q8 in range(8):
                    P.emit("scalar",
                           lambda sc, sl=sl, q8=q8: sc.activation(rsc[:, 0, :], xld[:, sl, q8 * 512:(q8 + 1) * 512], AF.Square, accum_out=stats[:, 1 + q8:2 + q8]),
                           reads=[xres], writes=[f"stq{q8}", "rp0"])
                P.emit("vector", lambda v: v.reduce_sum(stats[:, 9:10], stats[:, 1:9], axis=AX),
                       reads=[f"stq{q}" for q in range(8)], writes=["ssq"])
                P.emit("vector", lambda v: v.tensor_scalar(stats[:, 10:11], stats[:, 0:1], 1.0 / D, None, op0=ALU.mult),
                       reads=["st0"], writes=["mean"])
                P.emit("vector", lambda v: v.tensor_scalar(stats[:, 11:12], stats[:, 9:10], 1.0 / D, None, op0=ALU.mult),
                       reads=["ssq"], writes=["ex2"])
                P.emit("vector", lambda v: v.tensor_tensor(stats[:, 12:13], stats[:, 10:11], stats[:, 10:11], op=ALU.mult),
                       reads=["mean"], writes=["m2"])
                P.emit("vector", lambda v: v.tensor_tensor(stats[:, 13:14], stats[:, 11:12], stats[:, 12:13], op=ALU.subtract),
                       reads=["ex2", "m2"], writes=["var"])
                P.emit("vector", lambda v: v.tensor_scalar(stats[:, 13:14], stats[:, 13:14], EPS, None, op0=ALU.add),
                       reads=["var"], writes=["var"])
                P.emit("scalar", lambda sc: sc.activation(stats[:, 14:15], stats[:, 13:14], AF.Sqrt),
                       reads=["var"], writes=["sd"])
                P.emit("vector", lambda v: v.reciprocal(stats[:, 15:16], stats[:, 14:15]),
                       reads=["sd"], writes=["rstd"])
                P.emit("vector",
                       lambda v, sl=sl: v.tensor_scalar(ytm[:], xld[:, sl, :], stats[:, 10:11], stats[:, 15:16], op0=ALU.subtract, op1=ALU.mult),
                       reads=[xres, "mean", "rstd"], writes=["ytm"])
                for kc in range(KCH):
                    pz = kc % 2
                    P.emit("tensor",
                           lambda t, kc=kc, pz=pz: t.transpose(psTT[pz][:, 0:128], ytm[:, kc * 128:(kc + 1) * 128], ident[:]),
                           reads=["ytm", "ident"], writes=[f"psT{pz}"], tag="ytr")
                    P.emit("scalar",
                           lambda sc, kc=kc, tt=tt, pz=pz: sc.copy(out=big[:, kc, tt * 128:(tt + 1) * 128], in_=psTT[pz][:, 0:128]),
                           reads=[f"psT{pz}"], writes=[f"big_{tt}", f"psT{pz}"])

        def proj_resid(sub, wsrc, brow_src, ffn_delta):
            src = x_in if sub == 0 else master.ap()
            dst = out_t if sub == 3 else master.ap()
            if brow_src is not None:
                P.emit("gpsimd", lambda g, brow_src=brow_src: g.dma_start(out=brow_cur[:], in_=brow_src),
                       writes=["browc"], dma_sem=gp_s(), tag="brld")
            newdeps = {tt: [] for tt in range(NT)}
            for dg in range(16):
                wsl = dg % 2
                P.emit("gpsimd", lambda g, dg=dg, wsl=wsl: g.dma_start(out=wmv[:, wsl], in_=wsrc[dg]),
                       writes=[f"wmv{wsl}"], dma_sem=gp_s(), tag="wld")
                for tt in range(NT):
                    pp = (dg * NT + tt) % 2
                    ps = psAB[pp]

                    def mmo(t, dg=dg, tt=tt, wsl=wsl, ps=ps):
                        for kc in range(KCH):
                            t.matmul(ps[:, 0:256], big[:, kc, tt * 128:(tt + 1) * 128], wmv[:, wsl, kc, :],
                                     start=(kc == 0), stop=False)
                        if brow_src is not None:
                            last = t.matmul(ps[:, 0:256], onesr[:], brow_cur[0:1, dg * 256:(dg + 1) * 256],
                                            start=False, stop=not ffn_delta)
                        if ffn_delta:
                            last = t.matmul(ps[:, 0:256], ztT[:, tt * 128:(tt + 1) * 128], bsT[:, dg * 256:(dg + 1) * 256],
                                            start=False, stop=True)
                        return last
                    rds = [f"wmv{wsl}", f"big_{tt}"]
                    if brow_src is not None:
                        rds += ["onesr", "browc"]
                    if ffn_delta:
                        rds += [f"ztT_{tt}", "bsT"]
                    P.emit("tensor", mmo, reads=rds, writes=[f"ps{pp}"], tag="omm")
                    xsl = (dg * NT + tt) % 2
                    P.emit("sync",
                           lambda s, dg=dg, tt=tt, xsl=xsl, src=src: s.dma_start(out=xic[:, xsl, :], in_=src[tt * 128:(tt + 1) * 128, dg * 256:(dg + 1) * 256]),
                           writes=[f"xic{xsl}"], dma_sem=sy_s(), extra=mst_deps[tt], tag="xres")
                    P.emit("vector",
                           lambda v, xsl=xsl, ps=ps: v.tensor_tensor(xoc[:, xsl, :], ps[:, 0:256], xic[:, xsl, :], op=ALU.add),
                           reads=[f"ps{pp}", f"xic{xsl}"], writes=[f"xoc{xsl}", f"ps{pp}"], tag="resadd")
                    d = P.emit("sync",
                               lambda s, dg=dg, tt=tt, xsl=xsl, dst=dst: s.dma_start(out=dst[tt * 128:(tt + 1) * 128, dg * 256:(dg + 1) * 256], in_=xoc[:, xsl, :]),
                               reads=[f"xoc{xsl}"], dma_sem=sy_s(), tag="xst")
                    newdeps[tt].append(d)
            for tt in range(NT):
                mst_deps[tt] = newdeps[tt]

        def attn(sub, k_):
            sublayer_barrier()
            phase_y(sub)
            kst, vst, qst = [], [], []
            for ft in range(64):
                wsl = ft % 2
                P.emit("gpsimd", lambda g, ft=ft, wsl=wsl: g.dma_start(out=wmv[:, wsl, :, 0:128], in_=wqk[k_][ft]),
                       writes=[f"wmv{wsl}"], dma_sem=gp_s(), tag="wqkld")
                for th in range(2):
                    pp = (ft * 2 + th) % 2
                    ps = psAB[pp]

                    def mmqk(t, th=th, wsl=wsl, ps=ps):
                        for kc in range(KCH):
                            inst = t.matmul(ps[:], wmv[:, wsl, kc, 0:128], big[:, kc, th * 512:(th + 1) * 512],
                                            start=(kc == 0), stop=(kc == KCH - 1))
                        return inst
                    P.emit("tensor", mmqk,
                           reads=[f"wmv{wsl}"] + [f"big_{tt}" for tt in range(th * 4, th * 4 + 4)],
                           writes=[f"ps{pp}"], tag="qkmm")
                    esl = (ft * 2 + th) % 2
                    bres = "bqkA" if k_ == "a" else "bqkB"
                    if k_ == "a":
                        if ft < 32:
                            P.emit("vector",
                                   lambda v, ft=ft, ps=ps: v.tensor_scalar(qkpre[:], ps[:], bqk["a"][:, ft:ft + 1], ATT_SCALE, op0=ALU.add, op1=ALU.mult),
                                   reads=[f"ps{pp}", bres], writes=["qkpre", f"ps{pp}"], tag="qkev")
                        else:
                            P.emit("vector",
                                   lambda v, ft=ft, ps=ps: v.tensor_scalar(qkpre[:], ps[:], bqk["a"][:, ft:ft + 1], None, op0=ALU.add),
                                   reads=[f"ps{pp}", bres], writes=["qkpre", f"ps{pp}"], tag="qkev")
                        P.emit("tensor", lambda t: t.matmul(psD[:, 0:512], permB[:], qkpre[:], start=True, stop=True),
                               reads=["qkpre", "permB"], writes=["psD"], tag="permmm")
                        P.emit("vector", lambda v, th=th: v.tensor_tensor(rsc[:, 0, :], qkpre[:], cosb[:, th * 512:(th + 1) * 512], op=ALU.mult),
                               reads=["qkpre", "cosb"], writes=["rp0"], tag="ropec")
                        P.emit("vector", lambda v, th=th: v.tensor_tensor(rsc[:, 1, :], psD[:, 0:512], sinb[:, th * 512:(th + 1) * 512], op=ALU.mult),
                               reads=["psD", "sinb"], writes=["rp1", "psD"], tag="ropes")
                        P.emit("vector", lambda v, esl=esl: v.tensor_tensor(qkb[:, esl, :], rsc[:, 0, :], rsc[:, 1, :], op=ALU.add),
                               reads=["rp0", "rp1"], writes=[f"qkb{esl}"], tag="ropeadd")
                    else:
                        if ft < 32:
                            P.emit("vector",
                                   lambda v, ft=ft, esl=esl, ps=ps: v.tensor_scalar(qkb[:, esl, :], ps[:], bqk["b"][:, ft:ft + 1], ATT_SCALE, op0=ALU.add, op1=ALU.mult),
                                   reads=[f"ps{pp}", bres], writes=[f"qkb{esl}", f"ps{pp}"], tag="qkev")
                        else:
                            P.emit("vector",
                                   lambda v, ft=ft, esl=esl, ps=ps: v.tensor_scalar(qkb[:, esl, :], ps[:], bqk["b"][:, ft:ft + 1], None, op0=ALU.add),
                                   reads=[f"ps{pp}", bres], writes=[f"qkb{esl}", f"ps{pp}"], tag="qkev")
                    dst = q_dram.ap() if ft < 32 else contrib.ap()
                    row = (ft % 32) * 128
                    d = P.emit("sync",
                               lambda s, dst=dst, row=row, th=th, esl=esl: s.dma_start(out=dst[row:row + 128, th * 512:(th + 1) * 512], in_=qkb[:, esl, :]),
                               reads=[f"qkb{esl}"], dma_sem=sy_s(), tag="qkst")
                    (qst if ft < 32 else kst).append(d)
            P.emit("gpsimd", lambda g, k_=k_: g.dma_start(out=brow_cur[:], in_=brow_in[k_][0:1, :]),
                   writes=["browc"], dma_sem=gp_s(), tag="brld")
            for dg in range(16):
                wsl = dg % 2
                P.emit("gpsimd", lambda g, dg=dg, wsl=wsl: g.dma_start(out=wmv[:, wsl], in_=wv[k_][dg]),
                       writes=[f"wmv{wsl}"], dma_sem=gp_s(), tag="wvld")
                for tt in range(NT):
                    pp = (dg * NT + tt) % 2
                    ps = psAB[pp]

                    def mmv(t, dg=dg, tt=tt, wsl=wsl, ps=ps):
                        for kc in range(KCH):
                            t.matmul(ps[:, 0:256], big[:, kc, tt * 128:(tt + 1) * 128], wmv[:, wsl, kc, :],
                                     start=(kc == 0), stop=False)
                        return t.matmul(ps[:, 0:256], onesr[:], brow_cur[0:1, dg * 256:(dg + 1) * 256], start=False, stop=True)
                    P.emit("tensor", mmv,
                           reads=[f"wmv{wsl}", f"big_{tt}", "onesr", "browc"],
                           writes=[f"ps{pp}"], tag="vmm")
                    esl = (dg * NT + tt) % 2
                    P.emit("scalar", lambda sc, esl=esl, ps=ps: sc.copy(out=vev[:, esl, :], in_=ps[:, 0:256]),
                           reads=[f"ps{pp}"], writes=[f"vev{esl}", f"ps{pp}"], tag="vev")
                    vreg = contrib.ap()[D + tt * 512:D + (tt + 1) * 512, :].rearrange("(p x) c -> p x c", p=128)
                    d = P.emit("sync",
                               lambda s, vreg=vreg, dg=dg, esl=esl: s.dma_start(out=vreg[:, dg // 4, (dg % 4) * 256:(dg % 4) * 256 + 256], in_=vev[:, esl, :]),
                               reads=[f"vev{esl}"], dma_sem=sy_s(), tag="vstore")
                    vst.append(d)
            ccs = "cc1" if k_ == "a" else "cc2"
            P.barrier("gpsimd", kst + vst)
            agr = f"ag{k_}"

            def ccfn(g, k_=k_):
                return g.collective_compute("AllGather", ALU.bypass,
                                            replica_groups=[[0, 1, 2, 3], [4, 5, 6, 7]],
                                            ins=[contrib.ap()], outs=[ag[k_].ap()])
            P.mksem(ccs)
            P.semval[ccs] += 1
            ccomp = (ccs, 1)
            P.ops["gpsimd"].append(([], lambda g: ccfn(g), (ccs, 1), "cc"))
            P.res["contrib"] = {"w": None, "r": [ccomp]}
            P.res[agr] = {"w": ccomp, "r": []}
            P.barrier("sync", qst)
            for h in range(H):
                P.emit("sync", lambda s, h=h: s.dma_start(out=qTs[:, :], in_=q_dram.ap()[h * 128:(h + 1) * 128, :]),
                       writes=["qTs"], dma_sem=sy_s(), tag="qld")
                for r in range(4):
                    P.emit("sync", lambda s, h=h, r=r, k_=k_: s.dma_start(
                        out=Ksb[:, :, r, :],
                        in_=ag[k_].ap()[r * 2 * D + h * 128: r * 2 * D + (h + 1) * 128, :].rearrange("p (b c) -> p b c", c=256)),
                        reads=[agr], writes=["Ksb"], dma_sem=sy_s(), tag="kld")
                    P.emit("sync", lambda s, h=h, r=r, k_=k_: s.dma_start(
                        out=Vsb[:, r, :, :],
                        in_=ag[k_].ap()[r * 2 * D + D: r * 2 * D + 2 * D, :]
                            .rearrange("(m p x) c -> p m x c", p=128, x=4)[:, :, h // 8, (h % 8) * 128:(h % 8) * 128 + 128]),
                        reads=[agr], writes=["Vsb"], dma_sem=sy_s(), tag="vld")
                for pi in range(4):
                    K = pi + 1
                    nd = 2 * K
                    creg = (pi % 2) * 256
                    prev = None
                    for bc in range(nd):
                        kb, jh = bc // 2, bc % 2
                        sp = bc % 2
                        pss = psSbig if sp == 0 else psABbig
                        psres = ["psSbig"] if sp == 0 else ["ps0", "ps1"]
                        slot = bc % 2

                        def mms(t, kb=kb, jh=jh, pi=pi, pss=pss):
                            inst = None
                            for j in range(4):
                                r, hf = 2 * jh + j // 2, j % 2
                                inst = t.matmul(pss[:, j * 256:(j + 1) * 256], Ksb[:, kb, r, hf * 128:hf * 128 + 128],
                                                qTs[:, pi * 256:(pi + 1) * 256], start=True, stop=True)
                            return inst
                        P.emit("tensor", mms, reads=["Ksb", "qTs"], writes=psres, tag="smm")
                        P.emit("scalar", lambda sc, slot=slot, pss=pss: sc.activation(PT[:, slot, :], pss[:], AF.Exp),
                               reads=psres, writes=[f"PT{slot}"] + psres, tag="exp")
                        if kb == K - 1:
                            P.emit("gpsimd", lambda g, slot=slot, pi=pi, jh=jh: g.tensor_tensor(
                                PT[:, slot, :], PT[:, slot, :],
                                maskT[:, pi, 4 * jh:4 * jh + 4, :].rearrange("p a c -> p (a c)"),
                                op=ALU.mult),
                                reads=[f"PT{slot}", "maskT"], writes=[f"PT{slot}"], tag="mask")
                        if prev is not None:
                            pslot, pkb, pjh, pbc = prev

                            def mmpv(t, slot=pslot, pi=pi, bc=pbc, nd=nd, kb=pkb, jh=pjh, creg=creg):
                                inst = None
                                for j in range(4):
                                    r, hf = 2 * jh + j // 2, j % 2
                                    t.matmul(psD[0:1, 256:512], onesc[:], PT[:, slot, j * 256:(j + 1) * 256],
                                             start=(bc == 0 and j == 0), stop=(bc == nd - 1 and j == 3))
                                    inst = t.matmul(psCt[:, creg:creg + 256], Vsb[:, r, 2 * kb + hf, :], PT[:, slot, j * 256:(j + 1) * 256],
                                                    start=(bc == 0 and j == 0), stop=(bc == nd - 1 and j == 3))
                                return inst
                            P.emit("tensor", mmpv, reads=[f"PT{pslot}", "Vsb", "onesc", "psD", "psCt"],
                                   writes=["psD", "psCt", f"PT{pslot}"], tag="pv")
                        prev = (slot, kb, jh, bc)
                    pslot, pkb, pjh, pbc = prev

                    def mmpvL(t, slot=pslot, pi=pi, bc=pbc, nd=nd, kb=pkb, jh=pjh, creg=creg):
                        inst = None
                        for j in range(4):
                            r, hf = 2 * jh + j // 2, j % 2
                            t.matmul(psD[0:1, 256:512], onesc[:], PT[:, slot, j * 256:(j + 1) * 256],
                                     start=(bc == 0 and j == 0), stop=(bc == nd - 1 and j == 3))
                            inst = t.matmul(psCt[:, creg:creg + 256], Vsb[:, r, 2 * kb + hf, :], PT[:, slot, j * 256:(j + 1) * 256],
                                            start=(bc == 0 and j == 0), stop=(bc == nd - 1 and j == 3))
                        return inst
                    P.emit("tensor", mmpvL, reads=[f"PT{pslot}", "Vsb", "onesc", "psD", "psCt"],
                           writes=["psD", "psCt", f"PT{pslot}"], tag="pv")
                    P.emit("vector", lambda v: v.reciprocal(r1[0:1, 0:256], psD[0:1, 256:512]),
                           reads=["psD"], writes=["r1v", "psD"], tag="recip")
                    P.emit("tensor", lambda t: t.matmul(psD[:, 0:256], onesrf[:], r1[0:1, 0:256], start=True, stop=True),
                           reads=["r1v", "onesrf"], writes=["psD"], tag="rbc")
                    P.emit("scalar", lambda sc: sc.copy(out=recB[:, 0:256], in_=psD[:, 0:256]),
                           reads=["psD"], writes=["recBv", "psD"], tag="rcp")
                    P.emit("vector", lambda v, pi=pi, h=h, creg=creg: v.tensor_tensor(big[:, h, pi * 256:(pi + 1) * 256], psCt[:, creg:creg + 256], recB[:, 0:256], op=ALU.mult),
                           reads=["recBv", "psCt"], writes=[f"big_{tt}" for tt in range(NT)], tag="ctxev")
            proj_resid(sub, wo[k_], brow_in[k_][1:2, :], False)

        def ffn(sub):
            sublayer_barrier()
            phase_y(sub)
            for tt in range(NT):
                def mmg(t, tt=tt):
                    inst = None
                    for kc in range(KCH):
                        inst = t.matmul(psD[:, 0:36], big[:, kc, tt * 128:(tt + 1) * 128], gaT[:, kc, :],
                                        start=(kc == 0), stop=(kc == KCH - 1))
                    return inst
                P.emit("tensor", mmg, reads=[f"big_{tt}", "gaT"], writes=["psD"], tag="gmm")
                P.emit("vector", lambda v: v.tensor_copy(lg[:], psD[:, 0:36]), reads=["psD"], writes=["lg", "psD"])
                P.emit("vector", lambda v: v.tensor_tensor(lg[:, 0:4], lg[:, 0:4], ebc[:], op=ALU.add),
                       reads=["lg", "ebc"], writes=["lg"])
                P.emit("vector", lambda v: v.reduce_max(stats[:, 0:1], lg[:, 0:4], axis=AX), reads=["lg"], writes=["st0"])
                P.emit("vector", lambda v: v.tensor_scalar(mex[:, 0:4], lg[:, 0:4], stats[:, 0:1], None, op0=ALU.is_equal),
                       reads=["lg", "st0"], writes=["onehot"], tag="onehot")
                for e in range(4):
                    P.emit("vector", lambda v, e=e: v.tensor_scalar(zt[:, e * 8:(e + 1) * 8], lg[:, 4 + e * 8:4 + (e + 1) * 8], mex[:, e:e + 1], None, op0=ALU.mult),
                           reads=["lg", "onehot"], writes=[f"zte{e}"], tag="zmask")
                P.emit("tensor", lambda t: t.transpose(psT[0:32, 0:128], zt[:], ident[:]),
                       reads=[f"zte{e}" for e in range(4)] + ["ident"], writes=["psT0"], tag="ztr")
                P.emit("scalar", lambda sc, tt=tt: sc.copy(out=ztT[:, tt * 128:(tt + 1) * 128], in_=psT[0:32, 0:128]),
                       reads=["psT0"], writes=[f"ztT_{tt}", "psT0"])
            proj_resid(sub, wup_t, None, True)

        attn(0, "a")
        ffn(1)
        attn(2, "b")
        ffn(3)

        final = [(s, P.semval[s]) for s in SY + GP + ["cc1", "cc2"] if P.semval.get(s, 0) > 0]
        final += [(P.eng_sem[e], P.semval[P.eng_sem[e]]) for e in ENGS]
        P.barrier("sync", final)

        def runner(eng_name):
            def run(eng):
                for waits, fn, inc, tag in P.ops[eng_name]:
                    for (s, v) in waits:
                        eng.wait_ge(sems[s], v)
                    if fn is not None:
                        inst = fn(eng)
                        if inc is not None:
                            inst.then_inc(sems[inc[0]], inc[1])
            return run

        block.tensor(runner("tensor"))
        block.vector(runner("vector"))
        block.scalar(runner("scalar"))
        block.gpsimd(runner("gpsimd"))
        block.sync(runner("sync"))

    return nc


def _tile_moving(W):
    Fdim = W.shape[1]
    return np.ascontiguousarray(W.reshape(KCH, 128, Fdim // 256, 256).transpose(2, 1, 0, 3))


def _tile_stationary(W):
    Fdim = W.shape[1]
    return np.ascontiguousarray(W.reshape(KCH, 128, Fdim // 128, 128).transpose(2, 1, 0, 3))


def kernel(L0_attn_a_hidden_states, L0_attn_a_qkv_w, L0_attn_a_qkv_b, L0_attn_a_out_w, L0_attn_a_out_b,
           L0_attn_b_Wq, L0_attn_b_bq, L0_attn_b_Wk, L0_attn_b_bk, L0_attn_b_Wv, L0_attn_b_bv,
           L0_attn_b_Wo, L0_attn_b_bo, L0_ffn_W_up, L0_ffn_gate_w, L0_ffn_expert_bias,
           L0_ffn_lora_A, L0_ffn_lora_B):
    f32 = np.float32
    x = np.asarray(L0_attn_a_hidden_states, dtype=f32)
    qkv_w = np.asarray(L0_attn_a_qkv_w, dtype=f32)
    qkv_b = np.asarray(L0_attn_a_qkv_b, dtype=f32)

    wqk_a = _tile_stationary(qkv_w[:, 0:2 * D])
    wv_a = _tile_moving(qkv_w[:, 2 * D:3 * D])
    wo_a = _tile_moving(np.asarray(L0_attn_a_out_w, dtype=f32))
    wqk_b = _tile_stationary(np.ascontiguousarray(
        np.concatenate([np.asarray(L0_attn_b_Wq, dtype=f32).T, np.asarray(L0_attn_b_Wk, dtype=f32).T], axis=1)))
    wv_b = _tile_moving(np.ascontiguousarray(np.asarray(L0_attn_b_Wv, dtype=f32).T))
    wo_b = _tile_moving(np.ascontiguousarray(np.asarray(L0_attn_b_Wo, dtype=f32).T))
    wup = _tile_moving(np.ascontiguousarray(np.asarray(L0_ffn_W_up, dtype=f32).T))
    bqk_a = np.ascontiguousarray(qkv_b[0:2 * D].reshape(64, 128).T)
    bqk_b = np.ascontiguousarray(np.concatenate([np.asarray(L0_attn_b_bq, dtype=f32),
                                                 np.asarray(L0_attn_b_bk, dtype=f32)]).reshape(64, 128).T)
    brow_a = np.ascontiguousarray(np.stack([qkv_b[2 * D:3 * D], np.asarray(L0_attn_a_out_b, dtype=f32)]))
    brow_b = np.ascontiguousarray(np.stack([np.asarray(L0_attn_b_bv, dtype=f32), np.asarray(L0_attn_b_bo, dtype=f32)]))
    gate_w = np.asarray(L0_ffn_gate_w, dtype=f32)
    lora_A = np.asarray(L0_ffn_lora_A, dtype=f32)
    lora_B = np.asarray(L0_ffn_lora_B, dtype=f32)
    ga = np.concatenate([gate_w.T, lora_A.reshape(32, D).T], axis=1)
    gaT_h = np.ascontiguousarray(ga.reshape(KCH, 128, 36).transpose(1, 0, 2))
    bsT_h = np.ascontiguousarray(lora_B.transpose(0, 2, 1).reshape(32, D))
    eb_h = np.asarray(L0_ffn_expert_bias, dtype=f32).reshape(1, 4)
    ident_h = np.eye(128, dtype=f32)

    half = HD // 2
    inv_freq = (1.0 / (10000.0 ** (np.arange(half, dtype=np.float64) / half)))
    pos = np.arange(S, dtype=np.float64)
    ang = pos[None, :] * inv_freq[:, None]
    cos_full = np.concatenate([np.cos(ang), np.cos(ang)], axis=0).astype(f32)
    sin_full = np.concatenate([np.sin(ang), np.sin(ang)], axis=0).astype(f32)
    perm_h = np.zeros((128, 128), dtype=f32)
    for dd in range(64):
        perm_h[dd + 64, dd] = -1.0
        perm_h[dd, dd + 64] = 1.0

    shared = dict(wqk_a=wqk_a, wv_a=wv_a, wo_a=wo_a, wqk_b=wqk_b, wv_b=wv_b, wo_b=wo_b,
                  wup=wup, bqk_a=bqk_a, bqk_b=bqk_b, brow_a=brow_a, brow_b=brow_b,
                  gaT_in=gaT_h, bsT_in=bsT_h, eb_in=eb_h, ident_in=ident_h, perm_in=perm_h)

    nc = build_nc()
    in_maps = []
    for core in range(8):
        b, cp = core // 4, core % 4
        subs = [cp + 4 * m for m in range(NT)]
        toks = np.concatenate([np.arange(s_ * 128, s_ * 128 + 128) for s_ in subs])
        shard = np.ascontiguousarray(x[b, toks, :])
        cos_c = np.ascontiguousarray(cos_full[:, toks])
        sin_c = np.ascontiguousarray(sin_full[:, toks])
        m_h = np.zeros((4, 8, 128, 256), dtype=f32)
        for pi in range(4):
            for kc8 in range(8):
                r, hf = kc8 // 2, kc8 % 2
                ks = r + 4 * (2 * pi + hf)
                kpos = np.arange(ks * 128, ks * 128 + 128)
                q0 = cp + 4 * (2 * pi)
                q1 = cp + 4 * (2 * pi + 1)
                qpos = np.concatenate([np.arange(q0 * 128, q0 * 128 + 128),
                                       np.arange(q1 * 128, q1 * 128 + 128)])
                m_h[pi, kc8] = (kpos[:, None] <= qpos[None, :]).astype(f32)
        mask_h = np.ascontiguousarray(m_h.transpose(2, 0, 1, 3))
        d = dict(shared)
        d["x0"] = shard
        d["cos_in"] = cos_c
        d["sin_in"] = sin_c
        d["mask_in"] = mask_h
        in_maps.append(d)

    res = run_bass_kernel_spmd(nc, in_maps, core_ids=list(range(8)))

    out = np.empty((B, S, D), dtype=np.float32)
    for core in range(8):
        b, cp = core // 4, core % 4
        subs = [cp + 4 * m for m in range(NT)]
        shard = np.asarray(res.results[core]["xout"])
        for m, s_ in enumerate(subs):
            out[b, s_ * 128:(s_ + 1) * 128, :] = shard[m * 128:(m + 1) * 128, :]
    return out


# version 04 (final)
# speedup vs baseline: 1.1008x; 1.1008x over previous
import math
import numpy as np
import ml_dtypes
import concourse.bass as bass
import concourse.mybir as mybir
from concourse.bass_utils import run_bass_kernel_spmd

F32 = mybir.dt.float32
BF16 = mybir.dt.bfloat16
AF = mybir.ActivationFunctionType
ALU = mybir.AluOpType
AX = mybir.AxisListType.X

B, S, D, H, HD = 2, 4096, 4096, 32, 128
TPC, NT, KCH = 1024, 8, 32
ATT_SCALE = 1.0 / math.sqrt(HD)
EPS = 1e-5
ENGS = ("tensor", "vector", "scalar", "gpsimd", "sync")


class Prog:
    def __init__(self):
        self.ops = {e: [] for e in ENGS}
        self.semval = {}
        self.hiwater = {e: {} for e in ENGS}
        self.res = {}
        self.eng_sem = {"tensor": "pe", "vector": "ve", "scalar": "se",
                        "gpsimd": "po", "sync": "sy"}
        for s in self.eng_sem.values():
            self.mksem(s)

    def mksem(self, name):
        self.semval.setdefault(name, 0)

    def _need(self, eng, waits, dep):
        if dep is None:
            return
        sem, val = dep
        if val <= 0:
            return
        if eng == "tensor" and sem == "pe":
            return
        hw = self.hiwater[eng]
        if hw.get(sem, 0) >= val:
            return
        waits.append((sem, val))
        hw[sem] = val

    def emit(self, eng, fn, reads=(), writes=(), dma_sem=None, extra=(), tag=""):
        waits = []
        for r in reads:
            st = self.res.get(r)
            if st is not None:
                self._need(eng, waits, st["w"])
        for w in writes:
            st = self.res.get(w)
            if st is not None:
                self._need(eng, waits, st["w"])
                for rd in st["r"]:
                    self._need(eng, waits, rd)
        for d in extra:
            self._need(eng, waits, d)
        if dma_sem is not None:
            self.mksem(dma_sem)
            prev = self.semval[dma_sem]
            if prev:
                self._need(eng, waits, (dma_sem, prev))
            self.semval[dma_sem] += 16
            comp = (dma_sem, self.semval[dma_sem])
            inc = (dma_sem, 16)
        else:
            es = self.eng_sem[eng]
            self.semval[es] += 1
            comp = (es, self.semval[es])
            inc = (es, 1)
        self.ops[eng].append((waits, fn, inc, tag))
        for w in writes:
            self.res[w] = {"w": comp, "r": []}
        for r in reads:
            st = self.res.setdefault(r, {"w": None, "r": []})
            st["r"].append(comp)
        return comp

    def barrier(self, eng, deps):
        waits = []
        for d in deps:
            self._need(eng, waits, d)
        if waits:
            self.ops[eng].append((waits, None, None, "barrier"))


def build_nc():
    nc = bass.Bass("TRN2", target_bir_lowering=False)

    x_in = nc.dram_tensor("x0", [TPC, D], F32, kind="ExternalInput").ap()
    wqk = {"a": nc.dram_tensor("wqk_a", [64, 128, KCH, 128], F32, kind="ExternalInput").ap(),
           "b": nc.dram_tensor("wqk_b", [64, 128, KCH, 128], F32, kind="ExternalInput").ap()}
    wv = {"a": nc.dram_tensor("wv_a", [16, 128, KCH, 256], F32, kind="ExternalInput").ap(),
          "b": nc.dram_tensor("wv_b", [16, 128, KCH, 256], F32, kind="ExternalInput").ap()}
    wo = {"a": nc.dram_tensor("wo_a", [16, 128, KCH, 256], F32, kind="ExternalInput").ap(),
          "b": nc.dram_tensor("wo_b", [16, 128, KCH, 256], F32, kind="ExternalInput").ap()}
    wup_t = nc.dram_tensor("wup", [16, 128, KCH, 256], F32, kind="ExternalInput").ap()
    bqk_in = {"a": nc.dram_tensor("bqk_a", [128, 64], F32, kind="ExternalInput").ap(),
              "b": nc.dram_tensor("bqk_b", [128, 64], F32, kind="ExternalInput").ap()}
    brow_in = {"a": nc.dram_tensor("brow_a", [2, D], F32, kind="ExternalInput").ap(),
               "b": nc.dram_tensor("brow_b", [2, D], F32, kind="ExternalInput").ap()}
    gaT_in = nc.dram_tensor("gaT_in", [128, KCH, 36], F32, kind="ExternalInput").ap()
    bsT_in = nc.dram_tensor("bsT_in", [32, D], F32, kind="ExternalInput").ap()
    eb_in = nc.dram_tensor("eb_in", [1, 4], F32, kind="ExternalInput").ap()
    cos_in = nc.dram_tensor("cos_in", [128, TPC], F32, kind="ExternalInput").ap()
    sin_in = nc.dram_tensor("sin_in", [128, TPC], F32, kind="ExternalInput").ap()
    perm_in = nc.dram_tensor("perm_in", [128, 128], F32, kind="ExternalInput").ap()
    mask_in = nc.dram_tensor("mask_in", [128, 4, 8, 256], F32, kind="ExternalInput").ap()
    ident_in = nc.dram_tensor("ident_in", [128, 128], F32, kind="ExternalInput").ap()
    out_t = nc.dram_tensor("xout", [TPC, D], F32, kind="ExternalOutput").ap()

    master = nc.dram_tensor("master", [TPC, D], F32, kind="Internal")
    q_dram = nc.dram_tensor("q_dram", [D, TPC], BF16, kind="Internal")
    contrib = nc.dram_tensor("contrib", [2 * D, TPC], BF16, kind="Internal")
    ag = {"a": nc.dram_tensor("ag_a", [8 * D, TPC], BF16, kind="Internal"),
          "b": nc.dram_tensor("ag_b", [8 * D, TPC], BF16, kind="Internal")}

    P = Prog()
    SY = [f"dsy{i}" for i in range(8)]
    GP = [f"dgp{i}" for i in range(6)]
    SC = [f"dsc{i}" for i in range(6)]
    cnt = {"sy": 0, "gp": 0, "sc": 0}

    def sy_s():
        cnt["sy"] += 1
        return SY[cnt["sy"] % len(SY)]

    def gp_s():
        cnt["gp"] += 1
        return GP[cnt["gp"] % len(GP)]

    def sc_s():
        cnt["sc"] += 1
        return SC[cnt["sc"] % len(SC)]

    import contextlib
    with contextlib.ExitStack() as st_:
        def sb(name, shape, dt):
            return st_.enter_context(nc.sbuf_tensor(name, shape, dt))

        big = sb("big", [128, KCH, TPC], BF16)
        wmv = sb("wmv", [128, 2, KCH, 256], BF16)
        xld = sb("xld", [128, 1, D], F32)
        ytm = sb("ytm", [128, D], BF16)
        rsc = sb("rsc", [128, 2, 512], F32)
        qkb = sb("qkb", [128, 2, 512], BF16)
        vev = sb("vev", [128, 2, 256], BF16)
        cosb = sb("cosb", [128, TPC], F32)
        sinb = sb("sinb", [128, TPC], F32)
        permB = sb("permB", [128, 128], BF16)
        qkpre = sb("qkpre", [128, 512], BF16)
        maskT = sb("maskT", [128, 4, 8, 256], BF16)
        Ksb = sb("Ksb", [128, 4, 4, 256], BF16)
        Vsb = sb("Vsb", [128, 4, 8, 128], BF16)
        qTs = sb("qTs", [128, TPC], BF16)
        PT = sb("PT", [128, 2, 1024], BF16)
        recB = sb("recB", [128, 256], F32)
        r1 = sb("r1", [1, 256], F32)
        stats = sb("stats", [128, 16], F32)
        lg = sb("lg", [128, 36], F32)
        mex = sb("mex", [128, 4], F32)
        zt = sb("zt", [128, 32], BF16)
        ztT = sb("ztT", [32, TPC], BF16)
        bsT = sb("bsT", [32, D], BF16)
        gaT = sb("gaT", [128, KCH, 36], BF16)
        bqk = {"a": sb("bqkA", [128, 64], F32), "b": sb("bqkB", [128, 64], F32)}
        brow_cur = sb("brow_cur", [1, D], BF16)
        ebr = sb("ebr", [1, 4], F32)
        ebc = sb("ebc", [128, 4], F32)
        onesc = sb("onesc", [128, 1], BF16)
        onesr = sb("onesr", [1, 128], BF16)
        onesrf = sb("onesrf", [1, 128], F32)
        ident = sb("ident", [128, 128], BF16)
        xoc = sb("xoc", [128, 2, 256], F32)
        xic = sb("xic", [128, 2, 256], F32)

        psABbig = st_.enter_context(nc.psum_tensor("psABbig", [128, 1024], F32))
        psSbig = st_.enter_context(nc.psum_tensor("psSbig", [128, 1024], F32))
        psCt = st_.enter_context(nc.psum_tensor("psCt", [128, 512], F32))
        psD = st_.enter_context(nc.psum_tensor("psD", [128, 512], F32))
        psT = st_.enter_context(nc.psum_tensor("psT", [128, 1024], BF16))
        psT2 = st_.enter_context(nc.psum_tensor("psT2", [128, 1024], BF16))

        sems = {}
        for name in ["pe", "ve", "se", "po", "sy", "cc1", "cc2"] + SY + GP + SC:
            sems[name] = st_.enter_context(nc.semaphore(name))
        block = st_.enter_context(nc.Block())

        psAB = [psABbig[:, 0:512], psABbig[:, 512:1024]]
        psTT = [psT, psT2]

        P.emit("gpsimd", lambda g: g.dma_start(out=ident[:], in_=ident_in), writes=["ident"], dma_sem=gp_s())
        P.emit("gpsimd", lambda g: g.dma_start(out=permB[:], in_=perm_in), writes=["permB"], dma_sem=gp_s())
        P.emit("gpsimd", lambda g: g.dma_start(out=maskT[:], in_=mask_in), writes=["maskT"], dma_sem=gp_s())
        P.emit("gpsimd", lambda g: g.dma_start(out=gaT[:], in_=gaT_in), writes=["gaT"], dma_sem=gp_s())
        P.emit("gpsimd", lambda g: g.dma_start(out=bsT[:], in_=bsT_in), writes=["bsT"], dma_sem=gp_s())

        P.emit("sync", lambda s: s.dma_start(out=cosb[:], in_=cos_in), writes=["cosb"], dma_sem=sy_s())
        P.emit("sync", lambda s: s.dma_start(out=sinb[:], in_=sin_in), writes=["sinb"], dma_sem=sy_s())
        P.emit("sync", lambda s: s.dma_start(out=bqk["a"][:], in_=bqk_in["a"]), writes=["bqkA"], dma_sem=sy_s())
        P.emit("sync", lambda s: s.dma_start(out=bqk["b"][:], in_=bqk_in["b"]), writes=["bqkB"], dma_sem=sy_s())
        P.emit("sync", lambda s: s.dma_start(out=ebr[:], in_=eb_in), writes=["ebr"], dma_sem=sy_s())
        P.emit("vector", lambda v: v.memset(onesc[:], 1.0), writes=["onesc"])
        P.emit("vector", lambda v: v.memset(onesr[:], 1.0), writes=["onesr"])
        P.emit("vector", lambda v: v.memset(onesrf[:], 1.0), writes=["onesrf"])
        P.emit("tensor", lambda t: t.matmul(psD[:, 0:4], onesrf[:], ebr[:], start=True, stop=True),
               reads=["onesrf", "ebr"], writes=["psD"], tag="ebbc")
        P.emit("scalar", lambda sc: sc.copy(out=ebc[:], in_=psD[:, 0:4]), reads=["psD"], writes=["ebc", "psD"])

        mst_deps = {tt: [] for tt in range(NT)}

        def sublayer_barrier():
            deps = [(s, P.semval[s]) for s in SY + ["cc1", "cc2"] if P.semval.get(s, 0) > 0]
            P.barrier("sync", deps)

        def phase_y(sub):
            src = x_in if sub == 0 else master.ap()
            for tt in range(NT):
                sl = 0
                xres = "xld0"
                P.emit("sync",
                       lambda s, tt=tt, sl=sl, src=src: s.dma_start(out=xld[:, sl, :], in_=src[tt * 128:(tt + 1) * 128, :]),
                       writes=[xres], dma_sem=sy_s(), extra=mst_deps[tt], tag="xload")
                P.emit("vector", lambda v, sl=sl: v.reduce_sum(stats[:, 0:1], xld[:, sl, :], axis=AX),
                       reads=[xres], writes=["st0"])
                for q8 in range(8):
                    P.emit("scalar",
                           lambda sc, sl=sl, q8=q8: sc.activation(rsc[:, 0, :], xld[:, sl, q8 * 512:(q8 + 1) * 512], AF.Square, accum_out=stats[:, 1 + q8:2 + q8]),
                           reads=[xres], writes=[f"stq{q8}", "rp0"])
                P.emit("vector", lambda v: v.reduce_sum(stats[:, 9:10], stats[:, 1:9], axis=AX),
                       reads=[f"stq{q}" for q in range(8)], writes=["ssq"])
                P.emit("vector", lambda v: v.tensor_scalar(stats[:, 10:11], stats[:, 0:1], 1.0 / D, None, op0=ALU.mult),
                       reads=["st0"], writes=["mean"])
                P.emit("vector", lambda v: v.tensor_scalar(stats[:, 11:12], stats[:, 9:10], 1.0 / D, None, op0=ALU.mult),
                       reads=["ssq"], writes=["ex2"])
                P.emit("vector", lambda v: v.tensor_tensor(stats[:, 12:13], stats[:, 10:11], stats[:, 10:11], op=ALU.mult),
                       reads=["mean"], writes=["m2"])
                P.emit("vector", lambda v: v.tensor_tensor(stats[:, 13:14], stats[:, 11:12], stats[:, 12:13], op=ALU.subtract),
                       reads=["ex2", "m2"], writes=["var"])
                P.emit("vector", lambda v: v.tensor_scalar(stats[:, 13:14], stats[:, 13:14], EPS, None, op0=ALU.add),
                       reads=["var"], writes=["var"])
                P.emit("scalar", lambda sc: sc.activation(stats[:, 14:15], stats[:, 13:14], AF.Sqrt),
                       reads=["var"], writes=["sd"])
                P.emit("vector", lambda v: v.reciprocal(stats[:, 15:16], stats[:, 14:15]),
                       reads=["sd"], writes=["rstd"])
                P.emit("vector",
                       lambda v, sl=sl: v.tensor_scalar(ytm[:], xld[:, sl, :], stats[:, 10:11], stats[:, 15:16], op0=ALU.subtract, op1=ALU.mult),
                       reads=[xres, "mean", "rstd"], writes=["ytm"])
                for kc in range(KCH):
                    pz = kc % 2
                    P.emit("tensor",
                           lambda t, kc=kc, pz=pz: t.transpose(psTT[pz][:, 0:128], ytm[:, kc * 128:(kc + 1) * 128], ident[:]),
                           reads=["ytm", "ident"], writes=[f"psT{pz}"], tag="ytr")
                    P.emit("scalar",
                           lambda sc, kc=kc, tt=tt, pz=pz: sc.copy(out=big[:, kc, tt * 128:(tt + 1) * 128], in_=psTT[pz][:, 0:128]),
                           reads=[f"psT{pz}"], writes=[f"big_{tt}", f"psT{pz}"])

        def proj_resid(sub, wsrc, brow_src, ffn_delta):
            src = x_in if sub == 0 else master.ap()
            dst = out_t if sub == 3 else master.ap()
            if brow_src is not None:
                P.emit("gpsimd", lambda g, brow_src=brow_src: g.dma_start(out=brow_cur[:], in_=brow_src),
                       writes=["browc"], dma_sem=gp_s(), tag="brld")
            newdeps = {tt: [] for tt in range(NT)}
            for dg in range(16):
                wsl = dg % 2
                P.emit("gpsimd", lambda g, dg=dg, wsl=wsl: g.dma_start(out=wmv[:, wsl], in_=wsrc[dg]),
                       writes=[f"wmv{wsl}"], dma_sem=gp_s(), tag="wld")
                for tt in range(NT):
                    pp = (dg * NT + tt) % 2
                    ps = psAB[pp]

                    def mmo(t, dg=dg, tt=tt, wsl=wsl, ps=ps):
                        for kc in range(KCH):
                            t.matmul(ps[:, 0:256], big[:, kc, tt * 128:(tt + 1) * 128], wmv[:, wsl, kc, :],
                                     start=(kc == 0), stop=False)
                        if brow_src is not None:
                            last = t.matmul(ps[:, 0:256], onesr[:], brow_cur[0:1, dg * 256:(dg + 1) * 256],
                                            start=False, stop=not ffn_delta)
                        if ffn_delta:
                            last = t.matmul(ps[:, 0:256], ztT[:, tt * 128:(tt + 1) * 128], bsT[:, dg * 256:(dg + 1) * 256],
                                            start=False, stop=True)
                        return last
                    rds = [f"wmv{wsl}", f"big_{tt}"]
                    if brow_src is not None:
                        rds += ["onesr", "browc"]
                    if ffn_delta:
                        rds += [f"ztT_{tt}", "bsT"]
                    P.emit("tensor", mmo, reads=rds, writes=[f"ps{pp}"], tag="omm")
                    xsl = (dg * NT + tt) % 2
                    P.emit("scalar",
                           lambda s, dg=dg, tt=tt, xsl=xsl, src=src: s.dma_start(out=xic[:, xsl, :], in_=src[tt * 128:(tt + 1) * 128, dg * 256:(dg + 1) * 256]),
                           writes=[f"xic{xsl}"], dma_sem=sc_s(), extra=mst_deps[tt], tag="xres")
                    P.emit("vector",
                           lambda v, xsl=xsl, ps=ps: v.tensor_tensor(xoc[:, xsl, :], ps[:, 0:256], xic[:, xsl, :], op=ALU.add),
                           reads=[f"ps{pp}", f"xic{xsl}"], writes=[f"xoc{xsl}", f"ps{pp}"], tag="resadd")
                    d = P.emit("scalar",
                               lambda s, dg=dg, tt=tt, xsl=xsl, dst=dst: s.dma_start(out=dst[tt * 128:(tt + 1) * 128, dg * 256:(dg + 1) * 256], in_=xoc[:, xsl, :]),
                               reads=[f"xoc{xsl}"], dma_sem=sc_s(), tag="xst")
                    newdeps[tt].append(d)
            for tt in range(NT):
                mst_deps[tt] = newdeps[tt]

        def attn(sub, k_):
            sublayer_barrier()
            phase_y(sub)
            kst, vst, qst = [], [], []
            for ft in range(64):
                wsl = ft % 2
                P.emit("gpsimd", lambda g, ft=ft, wsl=wsl: g.dma_start(out=wmv[:, wsl, :, 0:128], in_=wqk[k_][ft]),
                       writes=[f"wmv{wsl}"], dma_sem=gp_s(), tag="wqkld")
                for th in range(2):
                    pp = (ft * 2 + th) % 2
                    ps = psAB[pp]

                    def mmqk(t, th=th, wsl=wsl, ps=ps):
                        for kc in range(KCH):
                            inst = t.matmul(ps[:], wmv[:, wsl, kc, 0:128], big[:, kc, th * 512:(th + 1) * 512],
                                            start=(kc == 0), stop=(kc == KCH - 1))
                        return inst
                    P.emit("tensor", mmqk,
                           reads=[f"wmv{wsl}"] + [f"big_{tt}" for tt in range(th * 4, th * 4 + 4)],
                           writes=[f"ps{pp}"], tag="qkmm")
                    esl = (ft * 2 + th) % 2
                    bres = "bqkA" if k_ == "a" else "bqkB"
                    if k_ == "a":
                        if ft < 32:
                            P.emit("vector",
                                   lambda v, ft=ft, ps=ps: v.tensor_scalar(qkpre[:], ps[:], bqk["a"][:, ft:ft + 1], ATT_SCALE, op0=ALU.add, op1=ALU.mult),
                                   reads=[f"ps{pp}", bres], writes=["qkpre", f"ps{pp}"], tag="qkev")
                        else:
                            P.emit("vector",
                                   lambda v, ft=ft, ps=ps: v.tensor_scalar(qkpre[:], ps[:], bqk["a"][:, ft:ft + 1], None, op0=ALU.add),
                                   reads=[f"ps{pp}", bres], writes=["qkpre", f"ps{pp}"], tag="qkev")
                        P.emit("tensor", lambda t: t.matmul(psD[:, 0:512], permB[:], qkpre[:], start=True, stop=True),
                               reads=["qkpre", "permB"], writes=["psD"], tag="permmm")
                        P.emit("vector", lambda v, th=th: v.tensor_tensor(rsc[:, 0, :], qkpre[:], cosb[:, th * 512:(th + 1) * 512], op=ALU.mult),
                               reads=["qkpre", "cosb"], writes=["rp0"], tag="ropec")
                        P.emit("vector", lambda v, th=th: v.tensor_tensor(rsc[:, 1, :], psD[:, 0:512], sinb[:, th * 512:(th + 1) * 512], op=ALU.mult),
                               reads=["psD", "sinb"], writes=["rp1", "psD"], tag="ropes")
                        P.emit("vector", lambda v, esl=esl: v.tensor_tensor(qkb[:, esl, :], rsc[:, 0, :], rsc[:, 1, :], op=ALU.add),
                               reads=["rp0", "rp1"], writes=[f"qkb{esl}"], tag="ropeadd")
                    else:
                        if ft < 32:
                            P.emit("vector",
                                   lambda v, ft=ft, esl=esl, ps=ps: v.tensor_scalar(qkb[:, esl, :], ps[:], bqk["b"][:, ft:ft + 1], ATT_SCALE, op0=ALU.add, op1=ALU.mult),
                                   reads=[f"ps{pp}", bres], writes=[f"qkb{esl}", f"ps{pp}"], tag="qkev")
                        else:
                            P.emit("vector",
                                   lambda v, ft=ft, esl=esl, ps=ps: v.tensor_scalar(qkb[:, esl, :], ps[:], bqk["b"][:, ft:ft + 1], None, op0=ALU.add),
                                   reads=[f"ps{pp}", bres], writes=[f"qkb{esl}", f"ps{pp}"], tag="qkev")
                    dst = q_dram.ap() if ft < 32 else contrib.ap()
                    row = (ft % 32) * 128
                    d = P.emit("sync",
                               lambda s, dst=dst, row=row, th=th, esl=esl: s.dma_start(out=dst[row:row + 128, th * 512:(th + 1) * 512], in_=qkb[:, esl, :]),
                               reads=[f"qkb{esl}"], dma_sem=sy_s(), tag="qkst")
                    (qst if ft < 32 else kst).append(d)
            P.emit("gpsimd", lambda g, k_=k_: g.dma_start(out=brow_cur[:], in_=brow_in[k_][0:1, :]),
                   writes=["browc"], dma_sem=gp_s(), tag="brld")
            for dg in range(16):
                wsl = dg % 2
                P.emit("gpsimd", lambda g, dg=dg, wsl=wsl: g.dma_start(out=wmv[:, wsl], in_=wv[k_][dg]),
                       writes=[f"wmv{wsl}"], dma_sem=gp_s(), tag="wvld")
                for tt in range(NT):
                    pp = (dg * NT + tt) % 2
                    ps = psAB[pp]

                    def mmv(t, dg=dg, tt=tt, wsl=wsl, ps=ps):
                        for kc in range(KCH):
                            t.matmul(ps[:, 0:256], big[:, kc, tt * 128:(tt + 1) * 128], wmv[:, wsl, kc, :],
                                     start=(kc == 0), stop=False)
                        return t.matmul(ps[:, 0:256], onesr[:], brow_cur[0:1, dg * 256:(dg + 1) * 256], start=False, stop=True)
                    P.emit("tensor", mmv,
                           reads=[f"wmv{wsl}", f"big_{tt}", "onesr", "browc"],
                           writes=[f"ps{pp}"], tag="vmm")
                    esl = (dg * NT + tt) % 2
                    P.emit("scalar", lambda sc, esl=esl, ps=ps: sc.copy(out=vev[:, esl, :], in_=ps[:, 0:256]),
                           reads=[f"ps{pp}"], writes=[f"vev{esl}", f"ps{pp}"], tag="vev")
                    vreg = contrib.ap()[D + tt * 512:D + (tt + 1) * 512, :].rearrange("(p x) c -> p x c", p=128)
                    d = P.emit("sync",
                               lambda s, vreg=vreg, dg=dg, esl=esl: s.dma_start(out=vreg[:, dg // 4, (dg % 4) * 256:(dg % 4) * 256 + 256], in_=vev[:, esl, :]),
                               reads=[f"vev{esl}"], dma_sem=sy_s(), tag="vstore")
                    vst.append(d)
            ccs = "cc1" if k_ == "a" else "cc2"
            P.barrier("gpsimd", kst + vst)
            agr = f"ag{k_}"

            def ccfn(g, k_=k_):
                return g.collective_compute("AllGather", ALU.bypass,
                                            replica_groups=[[0, 1, 2, 3], [4, 5, 6, 7]],
                                            ins=[contrib.ap()], outs=[ag[k_].ap()])
            P.mksem(ccs)
            P.semval[ccs] += 1
            ccomp = (ccs, 1)
            P.ops["gpsimd"].append(([], lambda g: ccfn(g), (ccs, 1), "cc"))
            P.res["contrib"] = {"w": None, "r": [ccomp]}
            P.res[agr] = {"w": ccomp, "r": []}
            P.barrier("sync", qst)
            for h in range(H):
                P.emit("sync", lambda s, h=h: s.dma_start(out=qTs[:, :], in_=q_dram.ap()[h * 128:(h + 1) * 128, :]),
                       writes=["qTs"], dma_sem=sy_s(), tag="qld")
                for r in range(4):
                    P.emit("sync", lambda s, h=h, r=r, k_=k_: s.dma_start(
                        out=Ksb[:, :, r, :],
                        in_=ag[k_].ap()[r * 2 * D + h * 128: r * 2 * D + (h + 1) * 128, :].rearrange("p (b c) -> p b c", c=256)),
                        reads=[agr], writes=["Ksb"], dma_sem=sy_s(), tag="kld")
                    P.emit("sync", lambda s, h=h, r=r, k_=k_: s.dma_start(
                        out=Vsb[:, r, :, :],
                        in_=ag[k_].ap()[r * 2 * D + D: r * 2 * D + 2 * D, :]
                            .rearrange("(m p x) c -> p m x c", p=128, x=4)[:, :, h // 8, (h % 8) * 128:(h % 8) * 128 + 128]),
                        reads=[agr], writes=["Vsb"], dma_sem=sy_s(), tag="vld")
                for pi in range(4):
                    K = pi + 1
                    nd = 2 * K
                    creg = (pi % 2) * 256
                    prev = None
                    for bc in range(nd):
                        kb, jh = bc // 2, bc % 2
                        sp = bc % 2
                        pss = psSbig if sp == 0 else psABbig
                        psres = ["psSbig"] if sp == 0 else ["ps0", "ps1"]
                        slot = bc % 2

                        def mms(t, kb=kb, jh=jh, pi=pi, pss=pss):
                            inst = None
                            for j in range(4):
                                r, hf = 2 * jh + j // 2, j % 2
                                inst = t.matmul(pss[:, j * 256:(j + 1) * 256], Ksb[:, kb, r, hf * 128:hf * 128 + 128],
                                                qTs[:, pi * 256:(pi + 1) * 256], start=True, stop=True)
                            return inst
                        P.emit("tensor", mms, reads=["Ksb", "qTs"], writes=psres, tag="smm")
                        P.emit("scalar", lambda sc, slot=slot, pss=pss: sc.activation(PT[:, slot, :], pss[:], AF.Exp),
                               reads=psres, writes=[f"PT{slot}"] + psres, tag="exp")
                        if kb == K - 1:
                            P.emit("gpsimd", lambda g, slot=slot, pi=pi, jh=jh: g.tensor_tensor(
                                PT[:, slot, :], PT[:, slot, :],
                                maskT[:, pi, 4 * jh:4 * jh + 4, :].rearrange("p a c -> p (a c)"),
                                op=ALU.mult),
                                reads=[f"PT{slot}", "maskT"], writes=[f"PT{slot}"], tag="mask")
                        if prev is not None:
                            pslot, pkb, pjh, pbc = prev

                            def mmpv(t, slot=pslot, pi=pi, bc=pbc, nd=nd, kb=pkb, jh=pjh, creg=creg):
                                inst = None
                                for j in range(4):
                                    r, hf = 2 * jh + j // 2, j % 2
                                    t.matmul(psD[0:1, 256:512], onesc[:], PT[:, slot, j * 256:(j + 1) * 256],
                                             start=(bc == 0 and j == 0), stop=(bc == nd - 1 and j == 3))
                                    inst = t.matmul(psCt[:, creg:creg + 256], Vsb[:, r, 2 * kb + hf, :], PT[:, slot, j * 256:(j + 1) * 256],
                                                    start=(bc == 0 and j == 0), stop=(bc == nd - 1 and j == 3))
                                return inst
                            P.emit("tensor", mmpv, reads=[f"PT{pslot}", "Vsb", "onesc", "psD", "psCt"],
                                   writes=["psD", "psCt", f"PT{pslot}"], tag="pv")
                        prev = (slot, kb, jh, bc)
                    pslot, pkb, pjh, pbc = prev

                    def mmpvL(t, slot=pslot, pi=pi, bc=pbc, nd=nd, kb=pkb, jh=pjh, creg=creg):
                        inst = None
                        for j in range(4):
                            r, hf = 2 * jh + j // 2, j % 2
                            t.matmul(psD[0:1, 256:512], onesc[:], PT[:, slot, j * 256:(j + 1) * 256],
                                     start=(bc == 0 and j == 0), stop=(bc == nd - 1 and j == 3))
                            inst = t.matmul(psCt[:, creg:creg + 256], Vsb[:, r, 2 * kb + hf, :], PT[:, slot, j * 256:(j + 1) * 256],
                                            start=(bc == 0 and j == 0), stop=(bc == nd - 1 and j == 3))
                        return inst
                    P.emit("tensor", mmpvL, reads=[f"PT{pslot}", "Vsb", "onesc", "psD", "psCt"],
                           writes=["psD", "psCt", f"PT{pslot}"], tag="pv")
                    P.emit("vector", lambda v: v.reciprocal(r1[0:1, 0:256], psD[0:1, 256:512]),
                           reads=["psD"], writes=["r1v", "psD"], tag="recip")
                    P.emit("tensor", lambda t: t.matmul(psD[:, 0:256], onesrf[:], r1[0:1, 0:256], start=True, stop=True),
                           reads=["r1v", "onesrf"], writes=["psD"], tag="rbc")
                    P.emit("scalar", lambda sc: sc.copy(out=recB[:, 0:256], in_=psD[:, 0:256]),
                           reads=["psD"], writes=["recBv", "psD"], tag="rcp")
                    P.emit("vector", lambda v, pi=pi, h=h, creg=creg: v.tensor_tensor(big[:, h, pi * 256:(pi + 1) * 256], psCt[:, creg:creg + 256], recB[:, 0:256], op=ALU.mult),
                           reads=["recBv", "psCt"], writes=[f"big_{tt}" for tt in range(NT)], tag="ctxev")
            proj_resid(sub, wo[k_], brow_in[k_][1:2, :], False)

        def ffn(sub):
            sublayer_barrier()
            phase_y(sub)
            for tt in range(NT):
                def mmg(t, tt=tt):
                    inst = None
                    for kc in range(KCH):
                        inst = t.matmul(psD[:, 0:36], big[:, kc, tt * 128:(tt + 1) * 128], gaT[:, kc, :],
                                        start=(kc == 0), stop=(kc == KCH - 1))
                    return inst
                P.emit("tensor", mmg, reads=[f"big_{tt}", "gaT"], writes=["psD"], tag="gmm")
                P.emit("vector", lambda v: v.tensor_copy(lg[:], psD[:, 0:36]), reads=["psD"], writes=["lg", "psD"])
                P.emit("vector", lambda v: v.tensor_tensor(lg[:, 0:4], lg[:, 0:4], ebc[:], op=ALU.add),
                       reads=["lg", "ebc"], writes=["lg"])
                P.emit("vector", lambda v: v.reduce_max(stats[:, 0:1], lg[:, 0:4], axis=AX), reads=["lg"], writes=["st0"])
                P.emit("vector", lambda v: v.tensor_scalar(mex[:, 0:4], lg[:, 0:4], stats[:, 0:1], None, op0=ALU.is_equal),
                       reads=["lg", "st0"], writes=["onehot"], tag="onehot")
                for e in range(4):
                    P.emit("vector", lambda v, e=e: v.tensor_scalar(zt[:, e * 8:(e + 1) * 8], lg[:, 4 + e * 8:4 + (e + 1) * 8], mex[:, e:e + 1], None, op0=ALU.mult),
                           reads=["lg", "onehot"], writes=[f"zte{e}"], tag="zmask")
                P.emit("tensor", lambda t: t.transpose(psT[0:32, 0:128], zt[:], ident[:]),
                       reads=[f"zte{e}" for e in range(4)] + ["ident"], writes=["psT0"], tag="ztr")
                P.emit("scalar", lambda sc, tt=tt: sc.copy(out=ztT[:, tt * 128:(tt + 1) * 128], in_=psT[0:32, 0:128]),
                       reads=["psT0"], writes=[f"ztT_{tt}", "psT0"])
            proj_resid(sub, wup_t, None, True)

        attn(0, "a")
        ffn(1)
        attn(2, "b")
        ffn(3)

        final = [(s, P.semval[s]) for s in SY + GP + SC + ["cc1", "cc2"] if P.semval.get(s, 0) > 0]
        final += [(P.eng_sem[e], P.semval[P.eng_sem[e]]) for e in ENGS]
        P.barrier("sync", final)

        def runner(eng_name):
            def run(eng):
                for waits, fn, inc, tag in P.ops[eng_name]:
                    for (s, v) in waits:
                        eng.wait_ge(sems[s], v)
                    if fn is not None:
                        inst = fn(eng)
                        if inc is not None:
                            inst.then_inc(sems[inc[0]], inc[1])
            return run

        block.tensor(runner("tensor"))
        block.vector(runner("vector"))
        block.scalar(runner("scalar"))
        block.gpsimd(runner("gpsimd"))
        block.sync(runner("sync"))

    return nc


def _tile_moving(W):
    Fdim = W.shape[1]
    return np.ascontiguousarray(W.reshape(KCH, 128, Fdim // 256, 256).transpose(2, 1, 0, 3))


def _tile_stationary(W):
    Fdim = W.shape[1]
    return np.ascontiguousarray(W.reshape(KCH, 128, Fdim // 128, 128).transpose(2, 1, 0, 3))


def kernel(L0_attn_a_hidden_states, L0_attn_a_qkv_w, L0_attn_a_qkv_b, L0_attn_a_out_w, L0_attn_a_out_b,
           L0_attn_b_Wq, L0_attn_b_bq, L0_attn_b_Wk, L0_attn_b_bk, L0_attn_b_Wv, L0_attn_b_bv,
           L0_attn_b_Wo, L0_attn_b_bo, L0_ffn_W_up, L0_ffn_gate_w, L0_ffn_expert_bias,
           L0_ffn_lora_A, L0_ffn_lora_B):
    f32 = np.float32
    x = np.asarray(L0_attn_a_hidden_states, dtype=f32)
    qkv_w = np.asarray(L0_attn_a_qkv_w, dtype=f32)
    qkv_b = np.asarray(L0_attn_a_qkv_b, dtype=f32)

    wqk_a = _tile_stationary(qkv_w[:, 0:2 * D])
    wv_a = _tile_moving(qkv_w[:, 2 * D:3 * D])
    wo_a = _tile_moving(np.asarray(L0_attn_a_out_w, dtype=f32))
    wqk_b = _tile_stationary(np.ascontiguousarray(
        np.concatenate([np.asarray(L0_attn_b_Wq, dtype=f32).T, np.asarray(L0_attn_b_Wk, dtype=f32).T], axis=1)))
    wv_b = _tile_moving(np.ascontiguousarray(np.asarray(L0_attn_b_Wv, dtype=f32).T))
    wo_b = _tile_moving(np.ascontiguousarray(np.asarray(L0_attn_b_Wo, dtype=f32).T))
    wup = _tile_moving(np.ascontiguousarray(np.asarray(L0_ffn_W_up, dtype=f32).T))
    bqk_a = np.ascontiguousarray(qkv_b[0:2 * D].reshape(64, 128).T)
    bqk_b = np.ascontiguousarray(np.concatenate([np.asarray(L0_attn_b_bq, dtype=f32),
                                                 np.asarray(L0_attn_b_bk, dtype=f32)]).reshape(64, 128).T)
    brow_a = np.ascontiguousarray(np.stack([qkv_b[2 * D:3 * D], np.asarray(L0_attn_a_out_b, dtype=f32)]))
    brow_b = np.ascontiguousarray(np.stack([np.asarray(L0_attn_b_bv, dtype=f32), np.asarray(L0_attn_b_bo, dtype=f32)]))
    gate_w = np.asarray(L0_ffn_gate_w, dtype=f32)
    lora_A = np.asarray(L0_ffn_lora_A, dtype=f32)
    lora_B = np.asarray(L0_ffn_lora_B, dtype=f32)
    ga = np.concatenate([gate_w.T, lora_A.reshape(32, D).T], axis=1)
    gaT_h = np.ascontiguousarray(ga.reshape(KCH, 128, 36).transpose(1, 0, 2))
    bsT_h = np.ascontiguousarray(lora_B.transpose(0, 2, 1).reshape(32, D))
    eb_h = np.asarray(L0_ffn_expert_bias, dtype=f32).reshape(1, 4)
    ident_h = np.eye(128, dtype=f32)

    half = HD // 2
    inv_freq = (1.0 / (10000.0 ** (np.arange(half, dtype=np.float64) / half)))
    pos = np.arange(S, dtype=np.float64)
    ang = pos[None, :] * inv_freq[:, None]
    cos_full = np.concatenate([np.cos(ang), np.cos(ang)], axis=0).astype(f32)
    sin_full = np.concatenate([np.sin(ang), np.sin(ang)], axis=0).astype(f32)
    perm_h = np.zeros((128, 128), dtype=f32)
    for dd in range(64):
        perm_h[dd + 64, dd] = -1.0
        perm_h[dd, dd + 64] = 1.0

    shared = dict(wqk_a=wqk_a, wv_a=wv_a, wo_a=wo_a, wqk_b=wqk_b, wv_b=wv_b, wo_b=wo_b,
                  wup=wup, bqk_a=bqk_a, bqk_b=bqk_b, brow_a=brow_a, brow_b=brow_b,
                  gaT_in=gaT_h, bsT_in=bsT_h, eb_in=eb_h, ident_in=ident_h, perm_in=perm_h)

    nc = build_nc()
    in_maps = []
    for core in range(8):
        b, cp = core // 4, core % 4
        subs = [cp + 4 * m for m in range(NT)]
        toks = np.concatenate([np.arange(s_ * 128, s_ * 128 + 128) for s_ in subs])
        shard = np.ascontiguousarray(x[b, toks, :])
        cos_c = np.ascontiguousarray(cos_full[:, toks])
        sin_c = np.ascontiguousarray(sin_full[:, toks])
        m_h = np.zeros((4, 8, 128, 256), dtype=f32)
        for pi in range(4):
            for kc8 in range(8):
                r, hf = kc8 // 2, kc8 % 2
                ks = r + 4 * (2 * pi + hf)
                kpos = np.arange(ks * 128, ks * 128 + 128)
                q0 = cp + 4 * (2 * pi)
                q1 = cp + 4 * (2 * pi + 1)
                qpos = np.concatenate([np.arange(q0 * 128, q0 * 128 + 128),
                                       np.arange(q1 * 128, q1 * 128 + 128)])
                m_h[pi, kc8] = (kpos[:, None] <= qpos[None, :]).astype(f32)
        mask_h = np.ascontiguousarray(m_h.transpose(2, 0, 1, 3))
        d = dict(shared)
        d["x0"] = shard
        d["cos_in"] = cos_c
        d["sin_in"] = sin_c
        d["mask_in"] = mask_h
        in_maps.append(d)

    res = run_bass_kernel_spmd(nc, in_maps, core_ids=list(range(8)))

    out = np.empty((B, S, D), dtype=np.float32)
    for core in range(8):
        b, cp = core // 4, core % 4
        subs = [cp + 4 * m for m in range(NT)]
        shard = np.asarray(res.results[core]["xout"])
        for m, s_ in enumerate(subs):
            out[b, s_ * 128:(s_ + 1) * 128, :] = shard[m * 128:(m + 1) * 128, :]
    return out
